# version 05
# baseline (speedup 1.0000x reference)
import numpy as np
import ml_dtypes
import concourse.bass as bass
import concourse.mybir as mybir
from concourse.bass_utils import run_bass_kernel_spmd
from contextlib import ExitStack

F32 = mybir.dt.float32
BF16 = mybir.dt.bfloat16
I32 = mybir.dt.int32
AF = mybir.ActivationFunctionType
ALU = mybir.AluOpType
AXX = mybir.AxisListType.X

D = 2048
DFF = 8192
H = 32
HD = 64
OWN = 2048
VT = 4096
TW = 128
NT_OWN = OWN // TW
NT_VT = VT // TW
NDC = D // TW
NFC = DFF // TW
NCH = VT // TW
NPAIR = H // 2
EPS_LN = 1e-5
EPS_Q = 1e-5
EPS_RMS = 1e-6

DEBUG = False
PHASE_LIMIT = 99
P0_LIST = "gud"
TVAR = 0
FA_SUB = 9
EMIT_LOG = None
NOCONST = 0
FFN_STAGE = 9
LN_SUB = 9
_CACHE = {}
_LAST_RESULTS = None
ENGINES = ("sync", "tensor", "vector", "scalar", "gpsimd")


class Sched:

    def __init__(self, nc, block, estack):
        self.nc = nc
        self.block = block
        self.estack = estack
        self.sems = {}
        for E in ENGINES:
            self.sems[E] = [estack.enter_context(nc.semaphore(f"prog_{E}")), 0]
        self.res = {}
        self.pending = {E: [] for E in ENGINES}
        self.chain = {}
        self.nbar = 0
        self.barwaits = {}

    def seq(self, eng_name, eng, instr):
        if eng_name not in self.chain:
            self.chain[eng_name] = [
                self.estack.enter_context(self.nc.semaphore(f"chain_{eng_name}")), 0]
        ent = self.chain[eng_name]
        ent[1] += 1
        instr.then_inc(ent[0], 1)
        eng.wait_ge(ent[0], ent[1])
        return instr

    def _dma_sem(self, key):
        k = f"D_{key}"
        if k not in self.sems:
            self.sems[k] = [self.estack.enter_context(self.nc.semaphore(k)), 0]
        return k

    def step(self, eng, fn, reads=(), writes=(), dma=0, dstore=False):
        waits = {}

        def add_wait(sk, wv):
            waits[sk] = max(waits.get(sk, 0), wv)

        bw = self.barwaits.pop(eng, None)
        if bw:
            for sk, wv in bw.items():
                add_wait(sk, wv)
        for r in reads:
            ent = self.res.get(r)
            if ent:
                for sk, wv in ent["w"].items():
                    add_wait(sk, wv)
        for r in writes:
            ent = self.res.get(r)
            if ent:
                for sk, wv in ent["w"].items():
                    add_wait(sk, wv)
                for sk, wv in ent["r"].items():
                    add_wait(sk, wv)
        if dma:
            key = (reads[0] if dstore else writes[0])
            sk_self = self._dma_sem(key)
            inc = 16 * dma
        else:
            sk_self = eng
            inc = 1
        self.sems[sk_self][1] += inc
        after = self.sems[sk_self][1]
        for r in reads:
            self.res.setdefault(r, {"w": {}, "r": {}})["r"][sk_self] = after
        for r in writes:
            ent = self.res.setdefault(r, {"w": {}, "r": {}})
            ent["w"][sk_self] = after
            ent["r"] = {}
        self.pending[eng].append((waits, fn, dma, sk_self))

    def flush(self):
        for E in ENGINES:
            steps = self.pending[E]
            if not steps:
                continue
            self.pending[E] = []
            sec = getattr(self.block, E)

            def emit(eng, _steps=steps, _E=E):
                for waits, fn, dma, sk_self in _steps:
                    if EMIT_LOG is not None:
                        ln = getattr(getattr(fn, "__code__", None), "co_firstlineno", 0)
                        EMIT_LOG.append(f"{_E} L{ln} waits={sorted(waits.items())} "
                                        f"dma={dma} inc={sk_self}")
                    for sk, wv in waits.items():
                        eng.wait_ge(self.sems[sk][0], wv)
                    if fn is None:
                        continue
                    ret = fn(eng)
                    instrs = ret if isinstance(ret, (list, tuple)) else [ret]
                    if dma:
                        assert len(instrs) == dma, (len(instrs), dma)
                        for ins in instrs:
                            ins.then_inc(self.sems[sk_self][0], 16)
                    else:
                        instrs[-1].then_inc(self.sems[sk_self][0], 1)
            sec(emit)

    def barrier(self):
        snap = {sk: v[1] for sk, v in self.sems.items() if v[1] > 0}
        for E in ENGINES:
            self.barwaits[E] = dict(snap)
        self.flush()

    def final_wait(self, E, reads):
        waits = {}
        bw = self.barwaits.pop(E, None)
        if bw:
            waits.update(bw)
        for r in reads:
            ent = self.res.get(r)
            if ent:
                for sk, wv in ent["w"].items():
                    waits[sk] = max(waits.get(sk, 0), wv)
        self.pending[E].append((waits, None, 0, None))
        self.flush()


def build_program():
    nc = bass.Bass("TRN2", target_bir_lowering=False, debug=False,
                   detect_race_conditions=False)
    skind = "ExternalOutput" if DEBUG else "Internal"

    def dt(n, s, d, k):
        return nc.dram_tensor(n, s, d, kind=k).ap()

    xpre = dt("xpre", [OWN, D], F32, "ExternalInput")
    xown = dt("xown", [OWN, D], F32, "ExternalInput")
    Wq = dt("Wq", [D, D], F32, "ExternalInput")
    Wk = dt("Wk", [D, D], F32, "ExternalInput")
    Wv = dt("Wv", [D, D], F32, "ExternalInput")
    Wg = dt("Wg", [D, D], F32, "ExternalInput")
    Wgk = dt("Wgk", [H, D], F32, "ExternalInput")
    Wo = dt("Wo", [D, D], F32, "ExternalInput")
    Wgate = dt("Wgate", [DFF, D], F32, "ExternalInput")
    Wup = dt("Wup", [DFF, D], F32, "ExternalInput")
    Wdown = dt("Wdown", [D, DFF], F32, "ExternalInput")
    identb = dt("identb", [TW, TW], BF16, "ExternalInput")
    identf = dt("identf", [TW, TW], F32, "ExternalInput")
    onesf = dt("onesf", [TW, TW], F32, "ExternalInput")
    maskTb = dt("maskTb", [TW, TW], BF16, "ExternalInput")
    maskTf = dt("maskTf", [TW, TW], F32, "ExternalInput")
    cosc = dt("cosc", [VT, H], F32, "ExternalInput")
    sinc = dt("sinc", [VT, H], F32, "ExternalInput")
    out = dt("out", [OWN, D], F32, "ExternalOutput")
    x1v = dt("x1v", [VT, D], F32, skind)
    x2s = dt("x2s", [OWN, D], F32, skind)
    h_s = dt("h_s", [OWN, DFF], F32, "Internal")
    iht_s = dt("iht_s", [TW, NT_OWN, DFF], BF16, "Internal")
    wgt_q = dt("wgt_q", [TW, DFF // 512, NDC, 512], BF16, "Internal")
    wut_q = dt("wut_q", [TW, DFF // 512, NDC, 512], BF16, "Internal")
    wdt_q = dt("wdt_q", [TW, D // 512, NFC, 512], BF16, "Internal")
    q_s = dt("q_s", [OWN, D], BF16, skind)
    k_s = dt("k_s", [VT, D], BF16, skind)
    v_s = dt("v_s", [VT, D], BF16, skind)
    sg_s = dt("sg_s", [OWN, D], F32, skind)
    o1_s = dt("o1_s", [OWN, D], F32, skind)
    dbg = dt("dbg", [TW, 4096], F32, skind)

    with ExitStack() as ctx:
        e = ctx.enter_context
        sb_identb = e(nc.sbuf_tensor("c_idb", [TW, TW], BF16))
        sb_identf = e(nc.sbuf_tensor("c_idf", [TW, TW], F32))
        sb_onesf = e(nc.sbuf_tensor("c_ones", [TW, TW], F32))
        sb_maskb = e(nc.sbuf_tensor("c_maskb", [TW, TW], BF16))
        sb_maskf = e(nc.sbuf_tensor("c_maskf", [TW, TW], F32))
        sb_wsc = e(nc.sbuf_tensor("c_wsc", [TW, 8], F32))
        sb_gkr = e(nc.sbuf_tensor("c_gkr", [TW, NT_VT * H], F32))
        sb_rof = e(nc.sbuf_tensor("c_rof", [TW, NT_VT * H], F32))
        sb_cof = e(nc.sbuf_tensor("c_cof", [TW, NT_VT * H], F32))
        sb_lam = e(nc.sbuf_tensor("c_lam", [TW, NT_VT * H], F32))

        block = e(nc.Block())
        sc = Sched(nc, block, ctx)

        def ldconst(eng):
            return [
                eng.dma_start(out=sb_identb[:], in_=identb),
                eng.dma_start(out=sb_identf[:], in_=identf),
                eng.dma_start(out=sb_onesf[:], in_=onesf),
                eng.dma_start(out=sb_maskb[:], in_=maskTb),
                eng.dma_start(out=sb_maskf[:], in_=maskTf),
            ]
        if not NOCONST:
            sc.step("sync", ldconst,
                    writes=("c_idb", "c_idf", "c_ones", "c_maskb", "c_maskf"), dma=5)

        RND_BIG = 2.0 ** 54
        RND_NHALF = -(2.0 ** 53)

        def emit_round(tag, src, scale_ap, dst, ncols, w, ybs, ibs, sbs, abs_,
                       reads, dst_res, clip=False, idx0=0):
            nset = len(ybs)
            for ci, c0 in enumerate(range(0, ncols, w)):
                si = (idx0 + ci) % nset
                yb, ib, sb, ab = ybs[si], ibs[si], sbs[si], abs_[si]
                sl = slice(c0, c0 + w)
                cw = slice(0, w)

                def s1(eng, sl=sl, cw=cw, yb=yb, ib=ib):
                    eng.activation(yb[:, cw], src[:, sl], AF.Identity, scale=scale_ap)
                    return eng.activation(ib[:, cw], src[:, sl], AF.Identity, scale=scale_ap)
                sc.step("scalar", s1, reads=reads, writes=(f"{tag}ry{si}", f"{tag}ri{si}"))
                sc.step("vector", lambda eng, cw=cw, yb=yb, ib=ib: eng.tensor_tensor(
                    yb[:, cw], yb[:, cw], ib[:, cw], ALU.subtract),
                    reads=(f"{tag}ry{si}", f"{tag}ri{si}"), writes=(f"{tag}ry{si}",))

                def s2(eng, cw=cw, yb=yb, sb=sb, ab=ab):
                    eng.activation(sb[:, cw], yb[:, cw], AF.Sign)
                    return eng.activation(ab[:, cw], yb[:, cw], AF.Abs, scale=RND_BIG)
                sc.step("scalar", s2, reads=(f"{tag}ry{si}",),
                        writes=(f"{tag}rs{si}", f"{tag}ra{si}"))

                def v2(eng, sl=sl, cw=cw, yb=yb, ib=ib, sb=sb, ab=ab):
                    q = lambda i: sc.seq("vector", eng, i)
                    q(eng.tensor_scalar(ab[:, cw], ab[:, cw], RND_NHALF, 0.0,
                                        ALU.add, ALU.max))
                    q(eng.scalar_tensor_tensor(ab[:, cw], ab[:, cw], 1.0, sb[:, cw],
                                               ALU.min, ALU.mult))
                    if clip:
                        q(eng.tensor_tensor(yb[:, cw], ib[:, cw], ab[:, cw], ALU.add))
                        return eng.tensor_scalar(dst[:, sl], yb[:, cw], 1.0, -1.0,
                                                 ALU.min, ALU.max)
                    return eng.tensor_tensor(dst[:, sl], ib[:, cw], ab[:, cw], ALU.add)
                sc.step("vector", v2,
                        reads=(f"{tag}ra{si}", f"{tag}rs{si}", f"{tag}ri{si}"),
                        writes=(f"{tag}ra{si}", f"{tag}ry{si}", dst_res))

        def p0_one(Wsrc, R, C, dst, slot, tag):
            NR = R // TW
            NCC = C // TW
            RW = min(C, 2048)
            with (
                nc.sbuf_tensor(f"{tag}w0", [TW, C], F32) as w0,
                nc.sbuf_tensor(f"{tag}w1", [TW, C], F32) as w1,
                nc.sbuf_tensor(f"{tag}acc", [TW, NR], F32) as acc,
                nc.sbuf_tensor(f"{tag}ry0", [TW, RW], F32) as ry0,
                nc.sbuf_tensor(f"{tag}ri0", [TW, RW], I32) as ri0,
                nc.sbuf_tensor(f"{tag}rs0", [TW, RW], F32) as rs0,
                nc.sbuf_tensor(f"{tag}ra0", [TW, RW], F32) as ra0,
                nc.sbuf_tensor(f"{tag}ry1", [TW, RW], F32) as ry1,
                nc.sbuf_tensor(f"{tag}ri1", [TW, RW], I32) as ri1,
                nc.sbuf_tensor(f"{tag}rs1", [TW, RW], F32) as rs1,
                nc.sbuf_tensor(f"{tag}ra1", [TW, RW], F32) as ra1,
                nc.sbuf_tensor(f"{tag}tb0", [TW, C], BF16) as tb0,
                nc.sbuf_tensor(f"{tag}tb1", [TW, C], BF16) as tb1,
                nc.sbuf_tensor(f"{tag}st0", [TW, 512], BF16) as stg0,
                nc.sbuf_tensor(f"{tag}st1", [TW, 512], BF16) as stg1,
                nc.sbuf_tensor(f"{tag}red", [TW, 4], F32) as red,
                nc.psum_tensor(f"{tag}ps", [TW, 512], F32) as ps,
                nc.psum_tensor(f"{tag}pb0", [TW, 512], F32) as pb0,
                nc.psum_tensor(f"{tag}pb1", [TW, 512], F32) as pb1,
            ):
                wt, tbb, stgb, pbb = [w0, w1], [tb0, tb1], [stg0, stg1], [pb0, pb1]
                for t in range(NR):
                    b = t % 2
                    sc.step("sync", lambda eng, t=t, b=b: eng.dma_start(
                        out=wt[b][:], in_=Wsrc[t * TW:(t + 1) * TW, :]),
                        writes=(f"{tag}w{b}",), dma=1)
                    sc.step("vector", lambda eng, t=t, b=b: eng.reduce_sum(
                        acc[:, t:t + 1], wt[b][:], axis=AXX, apply_absolute_value=True),
                        reads=(f"{tag}w{b}",), writes=(f"{tag}acc{t}",))
                sc.step("vector", lambda eng: eng.reduce_sum(red[:, 0:1], acc[:, 0:NR], axis=AXX),
                        reads=tuple(f"{tag}acc{t}" for t in range(NR)), writes=(f"{tag}red0",))
                sc.step("tensor", lambda eng: eng.matmul(ps[0:1, 0:TW], red[:, 0:1], sb_identf[:],
                                                         start=True, stop=True),
                        reads=(f"{tag}red0", "c_idf"), writes=(f"{tag}ps",))

                def fin(eng):
                    sc.seq("vector", eng, eng.reduce_sum(red[0:1, 1:2], ps[0:1, 0:TW], axis=AXX))
                    sc.seq("vector", eng, eng.tensor_scalar(red[0:1, 2:3], red[0:1, 1:2],
                                                            1.0 / (R * C), EPS_Q, ALU.mult, ALU.add))
                    return eng.reciprocal(red[0:1, 3:4], red[0:1, 2:3])
                sc.step("vector", fin, reads=(f"{tag}ps",), writes=(f"{tag}red",))
                sc.step("tensor", lambda eng: eng.matmul(
                    ps[:, 0:2], sb_onesf[0:1, :], red[0:1, 2:4], start=True, stop=True),
                    reads=(f"{tag}red", "c_ones"), writes=(f"{tag}ps",))
                sc.step("vector", lambda eng: eng.tensor_copy(
                    sb_wsc[:, 2 * slot:2 * slot + 2], ps[:, 0:2]),
                    reads=(f"{tag}ps",), writes=(f"wsc{slot}",))
                for t in range(NR):
                    b = t % 2
                    sc.step("sync", lambda eng, t=t, b=b: eng.dma_start(
                        out=wt[b][:], in_=Wsrc[t * TW:(t + 1) * TW, :]),
                        writes=(f"{tag}w{b}",), dma=1)
                    emit_round(tag, wt[b], sb_wsc[:, 2 * slot + 1:2 * slot + 2],
                               tbb[b], C, RW, [ry0, ry1], [ri0, ri1],
                               [rs0, rs1], [ra0, ra1],
                               reads=(f"{tag}w{b}", f"wsc{slot}"),
                               dst_res=f"{tag}tb{b}", clip=True,
                               idx0=t * (C // RW))
                    for g in range(NCC // 4):
                        gb = g % 2

                        def tr4(eng, b=b, g=g, gb=gb):
                            rr = None
                            for p4 in range(4):
                                cc = g * 4 + p4
                                rr = eng.matmul(pbb[gb][:, p4 * TW:(p4 + 1) * TW],
                                                tbb[b][:, cc * TW:(cc + 1) * TW], sb_identb[:],
                                                start=True, stop=True)
                            return rr
                        sc.step("tensor", tr4, reads=(f"{tag}tb{b}", "c_idb"), writes=(f"{tag}pb{gb}",))
                        sc.step("vector", lambda eng, gb=gb: eng.tensor_copy(stgb[gb][:], pbb[gb][:]),
                                reads=(f"{tag}pb{gb}",), writes=(f"{tag}stg{gb}",))
                        sc.step("scalar", lambda eng, t=t, g=g, gb=gb: eng.dma_start(
                            out=dst[:, (t * TW) // 512, g * 4:(g + 1) * 4,
                                    (t * TW) % 512:(t * TW) % 512 + TW],
                            in_=stgb[gb][:].rearrange("p (a b) -> p a b", b=TW)),
                            reads=(f"{tag}stg{gb}",), writes=(f"wq{slot}_{gb}",),
                            dma=1, dstore=True)
                sc.barrier()

        if "g" in P0_LIST:
            p0_one(Wgate, DFF, D, wgt_q, 0, "pg")
            sc.barrier()
        if PHASE_LIMIT <= 1:
            return nc
        if "u" in P0_LIST:
            p0_one(Wup, DFF, D, wut_q, 1, "pu")
            sc.barrier()
        if PHASE_LIMIT <= 2:
            return nc
        if "d" in P0_LIST:
            p0_one(Wdown, D, DFF, wdt_q, 2, "pd")
            sc.barrier()
        wq_all = tuple(f"wq{s}_{p}" for s in range(3) for p in range(2))

        def ln_steps(tag, xsrc_res, xbuf, scr, st):
            def _stats(eng):
                sc.seq("scalar", eng, eng.activation(scr[:], xbuf[:], AF.Square,
                                                     accum_out=st[:, 0:1]))
                return eng.activation(scr[:], xbuf[:], AF.Identity, accum_out=st[:, 1:2])
            sc.step("scalar", _stats, reads=(xsrc_res,), writes=(f"{tag}s01",))

            def stat1(eng):
                q = lambda i: sc.seq("vector", eng, i)
                q(eng.tensor_scalar(st[:, 2:3], st[:, 1:2], 1.0 / D, None, ALU.mult))
                q(eng.scalar_tensor_tensor(st[:, 3:4], st[:, 2:3], -1.0, st[:, 2:3], ALU.mult, ALU.mult))
                q(eng.tensor_scalar(st[:, 16:17], st[:, 0:1], 1.0 / D, EPS_LN, ALU.mult, ALU.add))
                q(eng.tensor_add(st[:, 4:5], st[:, 16:17], st[:, 3:4]))
                return eng.reciprocal(st[:, 5:6], st[:, 4:5])
            sc.step("vector", stat1, reads=(f"{tag}s01",), writes=(f"{tag}s5",))
            sc.step("scalar", lambda eng: eng.activation(st[:, 6:7], st[:, 5:6], AF.Sqrt),
                    reads=(f"{tag}s5",), writes=(f"{tag}s6",))

        def newton_rs(eng, st):
            q = lambda i: sc.seq("vector", eng, i)
            q(eng.tensor_mul(st[:, 7:8], st[:, 6:7], st[:, 6:7]))
            q(eng.tensor_mul(st[:, 7:8], st[:, 7:8], st[:, 4:5]))
            q(eng.tensor_scalar(st[:, 7:8], st[:, 7:8], -0.5, 1.5, ALU.mult, ALU.add))
            q(eng.tensor_mul(st[:, 8:9], st[:, 6:7], st[:, 7:8]))
            q(eng.scalar_tensor_tensor(st[:, 9:10], st[:, 2:3], -1.0, st[:, 8:9], ALU.mult, ALU.mult))

        def ffn(x_in, x_out, tag, out_res, in_res=()):
            NT = NT_OWN
            with ExitStack() as fstk:
                fe = fstk.enter_context
                st = fe(nc.sbuf_tensor(f"{tag}st", [TW, 20], F32))
                cgb = fe(nc.sbuf_tensor(f"{tag}cg", [TW, NT], F32))
                cub = fe(nc.sbuf_tensor(f"{tag}cu", [TW, NT], F32))
                chb = fe(nc.sbuf_tensor(f"{tag}ch", [TW, NT], F32))
                s_iT = ExitStack()
                iT = s_iT.enter_context(nc.sbuf_tensor(f"{tag}iT", [TW, NT * D], BF16))
                with ExitStack() as lstk:
                    le = lstk.enter_context
                    xb0 = le(nc.sbuf_tensor(f"{tag}x0", [TW, D], F32))
                    xb1 = le(nc.sbuf_tensor(f"{tag}x1", [TW, D], F32))
                    scr = le(nc.sbuf_tensor(f"{tag}scr", [TW, D], F32))
                    xnb = le(nc.sbuf_tensor(f"{tag}xn", [TW, D], F32))
                    i32b = le(nc.sbuf_tensor(f"{tag}i320", [TW, D], I32))
                    ryb = le(nc.sbuf_tensor(f"{tag}ry0", [TW, D], F32))
                    rsb = le(nc.sbuf_tensor(f"{tag}rs0", [TW, D], F32))
                    rab = le(nc.sbuf_tensor(f"{tag}ra0", [TW, D], F32))
                    i32c = le(nc.sbuf_tensor(f"{tag}i321", [TW, D], I32))
                    ryc = le(nc.sbuf_tensor(f"{tag}ry1", [TW, D], F32))
                    rsc = le(nc.sbuf_tensor(f"{tag}rs1", [TW, D], F32))
                    rac = le(nc.sbuf_tensor(f"{tag}ra1", [TW, D], F32))
                    ib0 = le(nc.sbuf_tensor(f"{tag}ib0", [TW, D], BF16))
                    ib1 = le(nc.sbuf_tensor(f"{tag}ib1", [TW, D], BF16))
                    pf0 = le(nc.psum_tensor(f"{tag}pf0", [TW, 512], F32))
                    pf1 = le(nc.psum_tensor(f"{tag}pf1", [TW, 512], F32))
                    xbb, ibb = [xb0, xb1], [ib0, ib1]
                    pfb = [pf0, pf1]
                    for t in range(NT):
                        b = t % 2
                        if LN_SUB >= 1:
                            sc.step("sync", lambda eng, t=t, b=b: eng.dma_start(
                                out=xbb[b][:], in_=x_in[t * TW:(t + 1) * TW, :]),
                                reads=in_res, writes=(f"{tag}x{b}",), dma=1)
                        if LN_SUB < 2:
                            continue
                        ln_steps(tag, f"{tag}x{b}", xbb[b], scr, st)
                        if LN_SUB < 3:
                            continue

                        def norm(eng, t=t, b=b):
                            q = lambda i: sc.seq("vector", eng, i)
                            newton_rs(eng, st)
                            q(eng.tensor_scalar(xnb[:], xbb[b][:], st[:, 8:9], st[:, 9:10],
                                                ALU.mult, ALU.add))
                            q(eng.reduce_max(st[:, 10:11], xnb[:], axis=AXX, apply_absolute_value=True))
                            q(eng.tensor_scalar(st[:, 10:11], st[:, 10:11], EPS_Q, None, ALU.max))
                            q(eng.tensor_scalar(st[:, 11:12], st[:, 10:11], 1.0 / 127.0, None, ALU.mult))
                            q(eng.reciprocal(st[:, 12:13], st[:, 11:12]))
                            eng.tensor_mul(cgb[:, t:t + 1], st[:, 11:12], sb_wsc[:, 0:1])
                            return eng.tensor_mul(cub[:, t:t + 1], st[:, 11:12], sb_wsc[:, 2:3])
                        sc.step("vector", norm,
                                reads=(f"{tag}s6", f"{tag}x{b}", "wsc0", "wsc1"),
                                writes=(f"{tag}xnw", f"{tag}s12", f"{tag}c{t}"))
                        emit_round(tag, xnb, st[:, 12:13], ibb[b], D, D,
                                   [ryb, ryc], [i32b, i32c], [rsb, rsc], [rab, rac],
                                   reads=(f"{tag}xnw", f"{tag}s12"),
                                   dst_res=f"{tag}ib{b}", idx0=t)
                        if LN_SUB < 4:
                            continue
                        for qtr in range(4):
                            def tr4m(eng, b=b, qtr=qtr):
                                rr = None
                                for j in range(4):
                                    dc = qtr * 4 + j
                                    rr = eng.matmul(pfb[qtr % 2][:, j * TW:(j + 1) * TW],
                                                    ibb[b][:, dc * TW:(dc + 1) * TW],
                                                    sb_identb[:], start=True, stop=True)
                                return rr
                            sc.step("tensor", tr4m, reads=(f"{tag}ib{b}", "c_idb"),
                                    writes=(f"{tag}pf{qtr % 2}",))
                            sc.step("vector", lambda eng, t=t, qtr=qtr: eng.tensor_copy(
                                iT[:, (t * NDC + qtr * 4) * TW:(t * NDC + (qtr + 1) * 4) * TW],
                                pfb[qtr % 2][:]),
                                reads=(f"{tag}pf{qtr % 2}",), writes=(f"{tag}iT{t}h{qtr % 2}",))
                    sc.barrier()
                iT_res = tuple(f"{tag}iT{t}h{h}" for t in range(NT) for h in range(2))
                c_res = tuple(f"{tag}c{t}" for t in range(NT))

                if FFN_STAGE < 2:
                    s_iT.close()
                    return (out_res,)
                with (
                    nc.sbuf_tensor(f"{tag}wg0", [TW, DFF], BF16) as wg0,
                    nc.sbuf_tensor(f"{tag}wg1", [TW, DFF], BF16) as wg1,
                    nc.sbuf_tensor(f"{tag}wu0", [TW, DFF], BF16) as wu0,
                    nc.sbuf_tensor(f"{tag}wu1", [TW, DFF], BF16) as wu1,
                    nc.sbuf_tensor(f"{tag}gb0", [TW, 512], F32) as gb0,
                    nc.sbuf_tensor(f"{tag}gb1", [TW, 512], F32) as gb1,
                    nc.sbuf_tensor(f"{tag}hb0", [TW, 512], F32) as hb0,
                    nc.sbuf_tensor(f"{tag}hb1", [TW, 512], F32) as hb1,
                    nc.psum_tensor(f"{tag}pG0", [TW, 512], F32) as pG0,
                    nc.psum_tensor(f"{tag}pG1", [TW, 512], F32) as pG1,
                    nc.psum_tensor(f"{tag}pU0", [TW, 512], F32) as pU0,
                    nc.psum_tensor(f"{tag}pU1", [TW, 512], F32) as pU1,
                ):
                    wgb, wub = [wg0, wg1], [wu0, wu1]
                    gbb, hbb = [gb0, gb1], [hb0, hb1]
                    pGb, pUb = [pG0, pG1], [pU0, pU1]
                    for fb in range(DFF // 512):
                        wb = fb % 2
                        sc.step("sync", lambda eng, fb=fb, wb=wb: [
                            eng.dma_start(out=wgb[wb][:].rearrange("p (a b) -> p a b", a=NDC),
                                          in_=wgt_q[:, fb, :, :]),
                            eng.dma_start(out=wub[wb][:].rearrange("p (a b) -> p a b", a=NDC),
                                          in_=wut_q[:, fb, :, :])],
                            reads=wq_all, writes=(f"{tag}wg{wb}", f"{tag}wu{wb}"), dma=2)
                        for t in range(NT):
                            pb = t % 2

                            def mm(eng, t=t, wb=wb, pb=pb):
                                for dc in range(NDC):
                                    eng.matmul(pGb[pb][:],
                                               iT[:, (t * NDC + dc) * TW:(t * NDC + dc + 1) * TW],
                                               wgb[wb][:, dc * 512:(dc + 1) * 512],
                                               start=(dc == 0), stop=(dc == NDC - 1))
                                rr = None
                                for dc in range(NDC):
                                    rr = eng.matmul(pUb[pb][:],
                                                    iT[:, (t * NDC + dc) * TW:(t * NDC + dc + 1) * TW],
                                                    wub[wb][:, dc * 512:(dc + 1) * 512],
                                                    start=(dc == 0), stop=(dc == NDC - 1))
                                return rr
                            if FA_SUB < 2:
                                continue
                            sc.step("tensor", mm,
                                    reads=iT_res + (f"{tag}wg{wb}", f"{tag}wu{wb}"),
                                    writes=(f"{tag}pG{pb}", f"{tag}pU{pb}"))
                            if FA_SUB < 3:
                                continue
                            if TVAR == 5:
                                sc.step("scalar", lambda eng, t=t, pb=pb: eng.activation(
                                    gbb[pb][:], pGb[pb][:], AF.Sigmoid, scale=1.0),
                                    reads=(f"{tag}pG{pb}",) + c_res, writes=(f"{tag}gb{pb}",))
                            elif TVAR == 6:
                                sc.step("scalar", lambda eng, t=t, pb=pb: eng.activation(
                                    gbb[pb][:], pGb[pb][:], AF.Identity, scale=cgb[:, t:t + 1]),
                                    reads=(f"{tag}pG{pb}",) + c_res, writes=(f"{tag}gb{pb}",))
                            elif TVAR == 7:
                                sc.step("vector", lambda eng, t=t, pb=pb: eng.tensor_scalar(
                                    hbb[pb][:], pGb[pb][:], cgb[:, t:t + 1], None, ALU.mult),
                                    reads=(f"{tag}pG{pb}",) + c_res, writes=(f"{tag}hb{pb}",))
                                sc.step("scalar", lambda eng, pb=pb: eng.activation(
                                    gbb[pb][:], hbb[pb][:], AF.Sigmoid),
                                    reads=(f"{tag}hb{pb}",), writes=(f"{tag}gb{pb}",))
                            else:
                                sc.step("scalar", lambda eng, t=t, pb=pb: eng.activation(
                                    gbb[pb][:], pGb[pb][:], AF.Sigmoid, scale=cgb[:, t:t + 1]),
                                    reads=(f"{tag}pG{pb}",) + c_res, writes=(f"{tag}gb{pb}",))
                            if FA_SUB < 4:
                                continue
                            sc.step("vector", lambda eng, t=t, pb=pb: eng.scalar_tensor_tensor(
                                hbb[pb][:], pUb[pb][:], cub[:, t:t + 1], gbb[pb][:],
                                ALU.mult, ALU.mult),
                                reads=(f"{tag}pU{pb}", f"{tag}gb{pb}"), writes=(f"{tag}hb{pb}",))
                            if FA_SUB < 5:
                                continue
                            sc.step("gpsimd", lambda eng, t=t, fb=fb, pb=pb: eng.dma_start(
                                out=h_s[t * TW:(t + 1) * TW, fb * 512:(fb + 1) * 512],
                                in_=hbb[pb][:]),
                                reads=(f"{tag}hb{pb}",), writes=(f"{tag}hsw{pb}",), dma=1, dstore=True)
                    sc.barrier()

                s_iT.close()
                if FFN_STAGE < 3:
                    return (out_res,)
                with (
                    nc.sbuf_tensor(f"{tag}h0", [TW, DFF], F32) as hh0,
                    nc.sbuf_tensor(f"{tag}h1", [TW, DFF], F32) as hh1,
                    nc.sbuf_tensor(f"{tag}hry0", [TW, 2048], F32) as hry0,
                    nc.sbuf_tensor(f"{tag}hri0", [TW, 2048], I32) as hri0,
                    nc.sbuf_tensor(f"{tag}hrs0", [TW, 2048], F32) as hrs0,
                    nc.sbuf_tensor(f"{tag}hra0", [TW, 2048], F32) as hra0,
                    nc.sbuf_tensor(f"{tag}hry1", [TW, 2048], F32) as hry1,
                    nc.sbuf_tensor(f"{tag}hri1", [TW, 2048], I32) as hri1,
                    nc.sbuf_tensor(f"{tag}hrs1", [TW, 2048], F32) as hrs1,
                    nc.sbuf_tensor(f"{tag}hra1", [TW, 2048], F32) as hra1,
                    nc.sbuf_tensor(f"{tag}ihb", [TW, DFF], BF16) as ihb,
                    nc.sbuf_tensor(f"{tag}ihs0", [TW, DFF // 2], BF16) as ihs0,
                    nc.sbuf_tensor(f"{tag}ihs1", [TW, DFF // 2], BF16) as ihs1,
                    nc.psum_tensor(f"{tag}qb0", [TW, 512], F32) as qb0,
                    nc.psum_tensor(f"{tag}qb1", [TW, 512], F32) as qb1,
                ):
                    hhb, ihsb, qbb = [hh0, hh1], [ihs0, ihs1], [qb0, qb1]
                    for t in range(NT):
                        b = t % 2
                        sc.step("sync", lambda eng, t=t, b=b: eng.dma_start(
                            out=hhb[b][:], in_=h_s[t * TW:(t + 1) * TW, :]),
                            reads=(f"{tag}hsw0", f"{tag}hsw1"), writes=(f"{tag}hh{b}",), dma=1)

                        def quant(eng, t=t, b=b):
                            q = lambda i: sc.seq("vector", eng, i)
                            q(eng.reduce_max(st[:, 13:14], hhb[b][:], axis=AXX, apply_absolute_value=True))
                            q(eng.tensor_scalar(st[:, 13:14], st[:, 13:14], EPS_Q, None, ALU.max))
                            q(eng.tensor_scalar(st[:, 14:15], st[:, 13:14], 1.0 / 127.0, None, ALU.mult))
                            q(eng.reciprocal(st[:, 15:16], st[:, 14:15]))
                            q(eng.tensor_mul(chb[:, t:t + 1], st[:, 14:15], sb_wsc[:, 4:5]))
                            return eng.tensor_scalar(chb[:, t:t + 1], chb[:, t:t + 1], 0.5, None, ALU.mult)
                        sc.step("vector", quant, reads=(f"{tag}hh{b}", "wsc2"),
                                writes=(f"{tag}s15", f"{tag}ch{t}"))
                        emit_round(f"{tag}h", hhb[b], st[:, 15:16], ihb, DFF, 2048,
                                   [hry0, hry1], [hri0, hri1], [hrs0, hrs1],
                                   [hra0, hra1],
                                   reads=(f"{tag}hh{b}", f"{tag}s15"),
                                   dst_res=f"{tag}ihb", idx0=t * 4)
                        for half in range(2):
                            for g in range(8):
                                def trq(eng, half=half, g=g):
                                    rr = None
                                    for j in range(4):
                                        fc = half * 32 + g * 4 + j
                                        rr = eng.matmul(qbb[g % 2][:, j * TW:(j + 1) * TW],
                                                        ihb[:, fc * TW:(fc + 1) * TW], sb_identb[:],
                                                        start=True, stop=True)
                                    return rr
                                sc.step("tensor", trq, reads=(f"{tag}ihb", "c_idb"),
                                        writes=(f"{tag}qb{g % 2}",))
                                sc.step("vector", lambda eng, half=half, g=g: eng.tensor_copy(
                                    ihsb[half][:, g * 512:(g + 1) * 512], qbb[g % 2][:]),
                                    reads=(f"{tag}qb{g % 2}",), writes=(f"{tag}ihsw{half}_{g}",))
                            sc.step("gpsimd", lambda eng, t=t, half=half: eng.dma_start(
                                out=iht_s[:, t, half * (DFF // 2):(half + 1) * (DFF // 2)],
                                in_=ihsb[half][:]),
                                reads=tuple(f"{tag}ihsw{half}_{g}" for g in range(8)),
                                writes=(f"{tag}ihtw",), dma=1, dstore=True)
                    sc.barrier()
                ch_res = tuple(f"{tag}ch{t}" for t in range(NT))

                if FFN_STAGE < 4:
                    return (out_res,)
                with (
                    nc.sbuf_tensor(f"{tag}wd", [TW, NFC * 512], BF16) as wdb,
                    nc.sbuf_tensor(f"{tag}it0", [TW, DFF], BF16) as it0,
                    nc.sbuf_tensor(f"{tag}it1", [TW, DFF], BF16) as it1,
                    nc.sbuf_tensor(f"{tag}xc0", [TW, 512], F32) as xc0,
                    nc.sbuf_tensor(f"{tag}xc1", [TW, 512], F32) as xc1,
                    nc.sbuf_tensor(f"{tag}ob0", [TW, 512], F32) as ob0,
                    nc.sbuf_tensor(f"{tag}ob1", [TW, 512], F32) as ob1,
                    nc.psum_tensor(f"{tag}pd0", [TW, 512], F32) as pd0,
                    nc.psum_tensor(f"{tag}pd1", [TW, 512], F32) as pd1,
                ):
                    itb, xcb, obb, pdb = [it0, it1], [xc0, xc1], [ob0, ob1], [pd0, pd1]
                    for ob in range(4):
                        sc.step("sync", lambda eng, ob=ob: eng.dma_start(
                            out=wdb[:].rearrange("p (a b) -> p a b", a=NFC),
                            in_=wdt_q[:, ob, :, :]),
                            reads=wq_all, writes=(f"{tag}wd",), dma=1)
                        for t in range(NT):
                            b = t % 2
                            sc.step("sync", lambda eng, t=t, b=b, ob=ob: [
                                eng.dma_start(out=itb[b][:], in_=iht_s[:, t, :]),
                                eng.dma_start(out=xcb[b][:],
                                              in_=x_in[t * TW:(t + 1) * TW, ob * 512:(ob + 1) * 512])],
                                reads=(f"{tag}ihtw",) + in_res,
                                writes=(f"{tag}it{b}", f"{tag}xc{b}"), dma=2)

                            def mmd(eng, b=b):
                                rr = None
                                for fc in range(NFC):
                                    rr = eng.matmul(pdb[b][:], itb[b][:, fc * TW:(fc + 1) * TW],
                                                    wdb[:, fc * 512:(fc + 1) * 512],
                                                    start=(fc == 0), stop=(fc == NFC - 1))
                                return rr
                            sc.step("tensor", mmd, reads=(f"{tag}it{b}", f"{tag}wd"),
                                    writes=(f"{tag}pd{b}",))
                            sc.step("vector", lambda eng, t=t, b=b: eng.scalar_tensor_tensor(
                                obb[b][:], pdb[b][:], chb[:, t:t + 1], xcb[b][:], ALU.mult, ALU.add),
                                reads=(f"{tag}pd{b}", f"{tag}xc{b}") + ch_res,
                                writes=(f"{tag}ob{b}",))
                            sc.step("gpsimd", lambda eng, t=t, ob=ob, b=b: eng.dma_start(
                                out=x_out[t * TW:(t + 1) * TW, ob * 512:(ob + 1) * 512],
                                in_=obb[b][:]),
                                reads=(f"{tag}ob{b}",), writes=(f"{out_res}",), dma=1, dstore=True)
                    sc.barrier()
            return (out_res,)

        if PHASE_LIMIT <= 3:
            return nc
        x1a_res = ffn(xpre, x1v[0:OWN, :], "fA", "x1a")
        sc.barrier()
        if PHASE_LIMIT <= 4:
            return nc
        x1b_res = ffn(xown, x1v[OWN:VT, :], "fB", "x1b")
        sc.barrier()

        def proj_half(row0, do_q, tag, deps):
            NT = NT_OWN
            toff = 0 if row0 == 0 else NT_OWN
            with ExitStack() as pstk:
                pe = pstk.enter_context
                xT = pe(nc.sbuf_tensor(f"{tag}xT", [TW, NT * D], BF16))
                st = pe(nc.sbuf_tensor(f"{tag}st", [TW, 20], F32))
                with ExitStack() as lstk:
                    le = lstk.enter_context
                    xb0 = le(nc.sbuf_tensor(f"{tag}x0", [TW, D], F32))
                    xb1 = le(nc.sbuf_tensor(f"{tag}x1", [TW, D], F32))
                    scr = le(nc.sbuf_tensor(f"{tag}scr", [TW, D], F32))
                    xnbf0 = le(nc.sbuf_tensor(f"{tag}xnb0", [TW, D], BF16))
                    xnbf1 = le(nc.sbuf_tensor(f"{tag}xnb1", [TW, D], BF16))
                    pb0 = le(nc.psum_tensor(f"{tag}pb0", [TW, 512], F32))
                    pb1 = le(nc.psum_tensor(f"{tag}pb1", [TW, 512], F32))
                    xbb, xnfb, pbb = [xb0, xb1], [xnbf0, xnbf1], [pb0, pb1]
                    for t in range(NT):
                        b = t % 2
                        sc.step("sync", lambda eng, t=t, b=b: eng.dma_start(
                            out=xbb[b][:], in_=x1v[row0 + t * TW:row0 + (t + 1) * TW, :]),
                            reads=deps, writes=(f"{tag}x{b}",), dma=1)
                        ln_steps(tag, f"{tag}x{b}", xbb[b], scr, st)

                        def norm(eng, b=b):
                            newton_rs(eng, st)
                            return eng.tensor_scalar(xnfb[b][:], xbb[b][:], st[:, 8:9], st[:, 9:10],
                                                     ALU.mult, ALU.add)
                        sc.step("vector", norm, reads=(f"{tag}s6", f"{tag}x{b}"),
                                writes=(f"{tag}xnb{b}",))
                        for qtr in range(4):
                            def tr4q(eng, b=b, qtr=qtr):
                                rr = None
                                for j in range(4):
                                    dc = qtr * 4 + j
                                    rr = eng.matmul(pbb[qtr % 2][:, j * TW:(j + 1) * TW],
                                                    xnfb[b][:, dc * TW:(dc + 1) * TW],
                                                    sb_identb[:], start=True, stop=True)
                                return rr
                            sc.step("tensor", tr4q, reads=(f"{tag}xnb{b}", "c_idb"),
                                    writes=(f"{tag}pb{qtr % 2}",))
                            sc.step("vector", lambda eng, t=t, qtr=qtr: eng.tensor_copy(
                                xT[:, (t * NDC + qtr * 4) * TW:(t * NDC + (qtr + 1) * 4) * TW],
                                pbb[qtr % 2][:]),
                                reads=(f"{tag}pb{qtr % 2}",), writes=(f"{tag}xT{t}h{qtr % 2}",))
                    sc.barrier()
                xT_res = tuple(f"{tag}xT{t}h{h}" for t in range(NT) for h in range(2))

                with (
                    nc.sbuf_tensor(f"{tag}pan", [TW, NDC * 4 * TW], BF16) as pan,
                    nc.sbuf_tensor(f"{tag}wn0", [TW, D], BF16) as wn0,
                    nc.sbuf_tensor(f"{tag}wn1", [TW, D], BF16) as wn1,
                    nc.sbuf_tensor(f"{tag}qf0", [TW, 512], F32) as qf0,
                    nc.sbuf_tensor(f"{tag}qf1", [TW, 512], F32) as qf1,
                    nc.sbuf_tensor(f"{tag}rot", [TW, 512], F32) as rot,
                    nc.sbuf_tensor(f"{tag}tmp", [TW, 512], F32) as tmpb,
                    nc.sbuf_tensor(f"{tag}obf0", [TW, 512], BF16) as obf0,
                    nc.sbuf_tensor(f"{tag}obf1", [TW, 512], BF16) as obf1,
                    nc.sbuf_tensor(f"{tag}sgf0", [TW, 512], F32) as sgf0,
                    nc.sbuf_tensor(f"{tag}sgf1", [TW, 512], F32) as sgf1,
                    nc.sbuf_tensor(f"{tag}cs", [TW, H], F32) as csb,
                    nc.sbuf_tensor(f"{tag}sn", [TW, H], F32) as snb,
                    nc.psum_tensor(f"{tag}pp0", [TW, 512], F32) as pp0,
                    nc.psum_tensor(f"{tag}pp1", [TW, 512], F32) as pp1,
                    nc.psum_tensor(f"{tag}tpb0", [TW, 512], F32) as tpb0,
                    nc.psum_tensor(f"{tag}tpb1", [TW, 512], F32) as tpb1,
                    nc.psum_tensor(f"{tag}pgk", [TW, H], F32) as pgk,
                ):
                    wnb = [wn0, wn1]
                    qfb, obfb, sgfb = [qf0, qf1], [obf0, obf1], [sgf0, sgf1]
                    ppb, tpbb = [pp0, pp1], [tpb0, tpb1]

                    def one_matrix(W, wtag, dst, dst_row0, rope, kind):
                        for ob in range(4):
                            for ot in range(4):
                                rt = ob * 4 + ot
                                b = rt % 2
                                sc.step("gpsimd", lambda eng, rt=rt, b=b: eng.dma_start(
                                    out=wnb[b][:], in_=W[rt * TW:(rt + 1) * TW, :]),
                                    writes=(f"{tag}wn{b}",), dma=1)
                                for g in range(4):
                                    def trp(eng, b=b, g=g):
                                        rr = None
                                        for j in range(4):
                                            dc = g * 4 + j
                                            rr = eng.matmul(tpbb[g % 2][:, j * TW:(j + 1) * TW],
                                                            wnb[b][:, dc * TW:(dc + 1) * TW],
                                                            sb_identb[:], start=True, stop=True)
                                        return rr
                                    sc.step("tensor", trp, reads=(f"{tag}wn{b}", "c_idb"),
                                            writes=(f"{tag}tpb{g % 2}",))

                                    def evp(eng, ot=ot, g=g):
                                        rr = None
                                        for j in range(4):
                                            dc = g * 4 + j
                                            rr = eng.tensor_copy(
                                                pan[:, (dc * 4 + ot) * TW:(dc * 4 + ot + 1) * TW],
                                                tpbb[g % 2][:, j * TW:(j + 1) * TW])
                                        return rr
                                    sc.step("vector", evp, reads=(f"{tag}tpb{g % 2}",),
                                            writes=(f"{wtag}pan{ot}g{g}",))
                            pres = tuple(f"{wtag}pan{o}g{g}" for o in range(4) for g in range(4))
                            for t in range(NT):
                                qb = t % 2

                                def mmp(eng, t=t, qb=qb):
                                    rr = None
                                    for dc in range(NDC):
                                        rr = eng.matmul(ppb[qb][:],
                                                        xT[:, (t * NDC + dc) * TW:(t * NDC + dc + 1) * TW],
                                                        pan[:, dc * 4 * TW:(dc + 1) * 4 * TW],
                                                        start=(dc == 0), stop=(dc == NDC - 1))
                                    return rr
                                sc.step("tensor", mmp, reads=xT_res + pres, writes=(f"{tag}pp{qb}",))
                                if kind == "silu":
                                    sc.step("scalar", lambda eng, qb=qb: eng.activation(
                                        sgfb[qb][:], ppb[qb][:], AF.Silu),
                                        reads=(f"{tag}pp{qb}",), writes=(f"{tag}sgf{qb}",))
                                    sc.step("gpsimd", lambda eng, t=t, ob=ob, qb=qb: eng.dma_start(
                                        out=dst[dst_row0 + t * TW:dst_row0 + (t + 1) * TW,
                                                ob * 512:(ob + 1) * 512],
                                        in_=sgfb[qb][:]),
                                        reads=(f"{tag}sgf{qb}",), writes=(f"{wtag}done",), dma=1, dstore=True)
                                elif rope:
                                    sc.step("vector", lambda eng, qb=qb: eng.tensor_copy(
                                        qfb[qb][:], ppb[qb][:]),
                                        reads=(f"{tag}pp{qb}",), writes=(f"{tag}qf{qb}",))
                                    sc.step("sync", lambda eng, t=t: [
                                        eng.dma_start(out=csb[:],
                                                      in_=cosc[row0 + t * TW:row0 + (t + 1) * TW, :]),
                                        eng.dma_start(out=snb[:],
                                                      in_=sinc[row0 + t * TW:row0 + (t + 1) * TW, :])],
                                        writes=(f"{tag}cs",), dma=2)

                                    def dorope(eng, qb=qb):
                                        q = qfb[qb]
                                        qv = q[:].rearrange("p (a two) -> p a two", two=2)
                                        rv = rot[:].rearrange("p (a two) -> p a two", two=2)
                                        sc.seq("vector", eng, eng.tensor_scalar(
                                            rv[:, :, 0:1], qv[:, :, 1:2], -1.0, None, ALU.mult))
                                        sc.seq("vector", eng, eng.tensor_copy(rv[:, :, 1:2], qv[:, :, 0:1]))
                                        csE = csb[:].unsqueeze(1).broadcast_to((TW, 8, H)) \
                                            .unsqueeze(3).broadcast_to((TW, 8, H, 2))
                                        snE = snb[:].unsqueeze(1).broadcast_to((TW, 8, H)) \
                                            .unsqueeze(3).broadcast_to((TW, 8, H, 2))
                                        q4 = q[:].rearrange("p (h i two) -> p h i two", i=H, two=2)
                                        r4 = rot[:].rearrange("p (h i two) -> p h i two", i=H, two=2)
                                        t4 = tmpb[:].rearrange("p (h i two) -> p h i two", i=H, two=2)
                                        sc.seq("vector", eng, eng.tensor_tensor(t4, q4, csE, ALU.mult))
                                        sc.seq("vector", eng, eng.tensor_tensor(r4, r4, snE, ALU.mult))
                                        return eng.tensor_add(obfb[qb][:], tmpb[:], rot[:])
                                    sc.step("vector", dorope,
                                            reads=(f"{tag}cs", f"{tag}qf{qb}"),
                                            writes=(f"{tag}obf{qb}",))
                                    sc.step("gpsimd", lambda eng, t=t, ob=ob, qb=qb: eng.dma_start(
                                        out=dst[dst_row0 + t * TW:dst_row0 + (t + 1) * TW,
                                                ob * 512:(ob + 1) * 512],
                                        in_=obfb[qb][:]),
                                        reads=(f"{tag}obf{qb}",), writes=(f"{wtag}done",), dma=1, dstore=True)
                                else:
                                    sc.step("vector", lambda eng, qb=qb: eng.tensor_copy(
                                        obfb[qb][:], ppb[qb][:]),
                                        reads=(f"{tag}pp{qb}",), writes=(f"{tag}obf{qb}",))
                                    sc.step("gpsimd", lambda eng, t=t, ob=ob, qb=qb: eng.dma_start(
                                        out=dst[dst_row0 + t * TW:dst_row0 + (t + 1) * TW,
                                                ob * 512:(ob + 1) * 512],
                                        in_=obfb[qb][:]),
                                        reads=(f"{tag}obf{qb}",), writes=(f"{wtag}done",), dma=1, dstore=True)
                        sc.flush()
                        return (f"{wtag}done",)

                    res = {}
                    res["k"] = one_matrix(Wk, f"{tag}K", k_s, row0, True, "bf16")
                    res["v"] = one_matrix(Wv, f"{tag}V", v_s, row0, False, "bf16")
                    if do_q:
                        res["q"] = one_matrix(Wq, f"{tag}Q", q_s, 0, True, "bf16")
                        res["g"] = one_matrix(Wg, f"{tag}G", sg_s, 0, False, "silu")

                    sc.step("gpsimd", lambda eng: eng.dma_start(
                        out=wnb[0][0:H, :], in_=Wgk[0:H, :]),
                        writes=(f"{tag}wn0",), dma=1)
                    for g in range(4):
                        def trg(eng, g=g):
                            rr = None
                            for j in range(4):
                                dc = g * 4 + j
                                rr = eng.matmul(tpbb[g % 2][:, j * H:(j + 1) * H],
                                                wnb[0][0:H, dc * TW:(dc + 1) * TW],
                                                sb_identb[0:H, 0:H], start=True, stop=True)
                            return rr
                        sc.step("tensor", trg, reads=(f"{tag}wn0", "c_idb"),
                                writes=(f"{tag}tpb{g % 2}",))

                        def evg(eng, g=g):
                            rr = None
                            for j in range(4):
                                dc = g * 4 + j
                                rr = eng.tensor_copy(pan[:, dc * H:(dc + 1) * H],
                                                     tpbb[g % 2][:, j * H:(j + 1) * H])
                            return rr
                        sc.step("vector", evg, reads=(f"{tag}tpb{g % 2}",), writes=(f"{tag}gkp{g}",))
                    gkp_res = tuple(f"{tag}gkp{g}" for g in range(4))
                    for t in range(NT):
                        def mgk(eng, t=t):
                            rr = None
                            for dc in range(NDC):
                                rr = eng.matmul(pgk[:, :],
                                                xT[:, (t * NDC + dc) * TW:(t * NDC + dc + 1) * TW],
                                                pan[:, dc * H:(dc + 1) * H],
                                                start=(dc == 0), stop=(dc == NDC - 1))
                            return rr
                        sc.step("tensor", mgk, reads=xT_res + gkp_res, writes=(f"{tag}pgk",))
                        sc.step("vector", lambda eng, t=t: eng.tensor_copy(
                            sb_gkr[:, (toff + t) * H:(toff + t + 1) * H], pgk[:, :]),
                            reads=(f"{tag}pgk",), writes=(f"gkr{toff + t}",))
                    sc.barrier()
            return res

        if PHASE_LIMIT <= 5:
            return nc
        pa = proj_half(0, False, "pA", x1a_res)
        sc.barrier()
        if PHASE_LIMIT <= 6:
            return nc
        pb = proj_half(OWN, True, "pB", x1b_res)
        sc.barrier()
        kv_res = pa["k"] + pa["v"] + pb["k"] + pb["v"]
        q_res = pb["q"]
        sg_res = pb["g"]
        gkr_res = tuple(f"gkr{t}" for t in range(NT_VT))

        if PHASE_LIMIT <= 7:
            return nc
        with (
            nc.sbuf_tensor("pgU", [TW, NT_VT * H], F32) as sb_u,
            nc.sbuf_tensor("pgG", [TW, NT_VT * H], F32) as sb_G,
            nc.sbuf_tensor("pgGT", [H, VT], F32) as sb_GT,
            nc.sbuf_tensor("pgRT", [H, VT], F32) as sb_RT,
            nc.sbuf_tensor("pgB", [H, NCH + 4], F32) as sb_B,
            nc.sbuf_tensor("pgrow", [1, NCH * H], F32) as sb_lrow,
            nc.psum_tensor("pgP0", [TW, H], F32) as pgp0,
            nc.psum_tensor("pgP1", [TW, 512], F32) as pgp1,
        ):
            def sp1(eng):
                sc.seq("scalar", eng, eng.activation(sb_u[:], sb_gkr[:], AF.Exp, scale=-1.0))
                return eng.activation(sb_u[:], sb_u[:], AF.Ln, bias=1.0)
            sc.step("scalar", sp1, reads=gkr_res, writes=("pg_u",))
            for ti in range(NT_VT):
                def csum(eng, ti=ti):
                    rr = None
                    for tj in range(ti + 1):
                        lhs = sb_maskf if tj == ti else sb_onesf
                        rr = eng.matmul(pgp0[:, :], lhs[:],
                                        sb_u[:, tj * H:(tj + 1) * H],
                                        start=(tj == 0), stop=(tj == ti))
                    return rr
                sc.step("tensor", csum, reads=("pg_u", "c_ones", "c_maskf"), writes=("pg_p0",))
                sc.step("vector", lambda eng, ti=ti: eng.tensor_copy(
                    sb_G[:, ti * H:(ti + 1) * H], pgp0[:, :]),
                    reads=("pg_p0",), writes=(f"pg_G{ti}",))
                sc.step("tensor", lambda eng, ti=ti: eng.matmul(
                    pgp1[0:H, 0:TW], sb_G[:, ti * H:(ti + 1) * H], sb_identf[:],
                    start=True, stop=True),
                    reads=(f"pg_G{ti}", "c_idf"), writes=("pg_p1",))
                sc.step("vector", lambda eng, ti=ti: eng.tensor_copy(
                    sb_GT[:, ti * TW:(ti + 1) * TW], pgp1[0:H, 0:TW]),
                    reads=("pg_p1",), writes=(f"pg_GT{ti}",))
            GT_res = tuple(f"pg_GT{t}" for t in range(NT_VT))
            def bnd(eng):
                eng.memset(sb_B[:, 0:1], 0.0)
                return eng.tensor_scalar(
                    sb_B[:, 1:NCH + 1],
                    sb_GT[:].rearrange("h (c w) -> h c w", w=TW)[:, :, TW - 1:TW]
                    .rearrange("h c w -> h (c w)"),
                    1.0 / 16.0, None, ALU.mult)
            sc.step("vector", bnd, reads=GT_res, writes=("pg_B",))
            for c in range(NCH):
                sc.step("scalar", lambda eng, c=c: eng.activation(
                    sb_RT[:, c * TW:(c + 1) * TW], sb_GT[:, c * TW:(c + 1) * TW],
                    AF.Exp, scale=-1.0 / 16.0, bias=sb_B[:, c:c + 1]),
                    reads=("pg_B",) + GT_res, writes=(f"pg_RT{c}",))
                sc.step("tensor", lambda eng, c=c: eng.matmul(
                    pgp1[:, 0:H], sb_RT[0:H, c * TW:(c + 1) * TW], sb_identf[0:H, 0:H],
                    start=True, stop=True),
                    reads=(f"pg_RT{c}", "c_idf"), writes=("pg_p1",))
                sc.step("vector", lambda eng, c=c: eng.tensor_copy(
                    sb_rof[:, c * H:(c + 1) * H], pgp1[:, 0:H]),
                    reads=("pg_p1",), writes=(f"pg_rof{c}",))
            rof_res = tuple(f"pg_rof{c}" for c in range(NCH))
            sc.step("vector", lambda eng: eng.reciprocal(sb_cof[:], sb_rof[:]),
                    reads=rof_res, writes=("pg_cof",))
            sc.step("gpsimd", lambda eng: eng.dma_start(
                out=sb_lrow[0:1, :], in_=sb_rof[TW - 1:TW, :]),
                reads=rof_res, writes=("pg_lrow",), dma=1, dstore=True)

            def lamm(eng):
                rr = None
                for half in range(2):
                    rr = eng.matmul(pgp1[:, 0:512], sb_onesf[0:1, :],
                                    sb_lrow[0:1, half * 512:(half + 1) * 512],
                                    start=True, stop=True)
                    break
                return rr
            sc.step("tensor", lamm, reads=("pg_lrow", "c_ones"), writes=("pg_p1",))
            sc.step("vector", lambda eng: eng.tensor_copy(sb_lam[:, 0:512], pgp1[:, 0:512]),
                    reads=("pg_p1",), writes=("pg_lam0",))
            sc.step("tensor", lambda eng: eng.matmul(
                pgp1[:, 0:512], sb_onesf[0:1, :], sb_lrow[0:1, 512:1024], start=True, stop=True),
                reads=("pg_lrow", "c_ones"), writes=("pg_p1",))
            sc.step("vector", lambda eng: eng.tensor_copy(sb_lam[:, 512:1024], pgp1[:, 0:512]),
                    reads=("pg_p1",), writes=("pg_lam1",))
            if DEBUG:
                def dbgdump(eng):
                    return [eng.dma_start(out=dbg[:, 0:1024], in_=sb_G[:]),
                            eng.dma_start(out=dbg[:, 1024:2048], in_=sb_rof[:]),
                            eng.dma_start(out=dbg[:, 2048:3072], in_=sb_cof[:]),
                            eng.dma_start(out=dbg[:, 3072:4096], in_=sb_lam[:])]
                sc.step("gpsimd", dbgdump,
                        reads=tuple(f"pg_G{t}" for t in range(NT_VT)) + rof_res
                        + ("pg_cof", "pg_lam0", "pg_lam1"),
                        writes=("pg_dbg",), dma=4, dstore=True)
            sc.barrier()

        fac_res = ("pg_cof", "pg_lam0", "pg_lam1") + tuple(f"pg_rof{c}" for c in range(NCH))

        if PHASE_LIMIT <= 8:
            return nc
        with ExitStack() as rstk:
            re_ = rstk.enter_context
            kb0 = re_(nc.sbuf_tensor("r_k0", [TW, D], BF16))
            kb1 = re_(nc.sbuf_tensor("r_k1", [TW, D], BF16))
            vb0 = re_(nc.sbuf_tensor("r_v0", [TW, D], BF16))
            vb1 = re_(nc.sbuf_tensor("r_v1", [TW, D], BF16))
            qb0_ = re_(nc.sbuf_tensor("r_q0", [TW, D], BF16))
            qb1_ = re_(nc.sbuf_tensor("r_q1", [TW, D], BF16))
            ktl = re_(nc.sbuf_tensor("r_kt", [TW, D], BF16))
            kTb = re_(nc.sbuf_tensor("r_kT", [TW, D], BF16))
            qTb = re_(nc.sbuf_tensor("r_qT", [TW, D], BF16))
            ATb = re_(nc.sbuf_tensor("r_AT", [TW, H * TW], BF16))
            Sp = re_(nc.sbuf_tensor("r_Sp", [TW, NPAIR * TW], F32))
            Sb = re_(nc.sbuf_tensor("r_Sb", [TW, NPAIR * TW], BF16))
            o1b0 = re_(nc.sbuf_tensor("r_o0", [TW, D], F32))
            o1b1 = re_(nc.sbuf_tensor("r_o1", [TW, D], F32))
            pS = re_(nc.psum_tensor("r_pS", [TW, 512], F32))
            pO0 = re_(nc.psum_tensor("r_pO0", [TW, 512], F32))
            pO1 = re_(nc.psum_tensor("r_pO1", [TW, 512], F32))
            pO2 = re_(nc.psum_tensor("r_pO2", [TW, 512], F32))
            pO3 = re_(nc.psum_tensor("r_pO3", [TW, 512], F32))
            pD0 = re_(nc.psum_tensor("r_pD0", [TW, 512], F32))
            pD1 = re_(nc.psum_tensor("r_pD1", [TW, 512], F32))
            pT = re_(nc.psum_tensor("r_pT", [TW, 512], F32))
            kbb, vbb, qbb2 = [kb0, kb1], [vb0, vb1], [qb0_, qb1_]
            pOb = [pO0, pO1, pO2, pO3]
            pDb = [pD0, pD1]
            o1bb = [o1b0, o1b1]

            def init_state(eng):
                eng.memset(Sp[:], 0.0)
                return eng.memset(Sb[:], 0.0)
            sc.step("vector", init_state, writes=("r_Sp", "r_Sb"))

            for c in range(NCH):
                own = c >= NCH // 2
                b = c % 2
                sc.step("sync", lambda eng, c=c, b=b: [
                    eng.dma_start(out=kbb[b][:], in_=k_s[c * TW:(c + 1) * TW, :]),
                    eng.dma_start(out=vbb[b][:], in_=v_s[c * TW:(c + 1) * TW, :])],
                    reads=kv_res, writes=(f"r_k{b}", f"r_v{b}"), dma=2)
                if own:
                    sc.step("sync", lambda eng, c=c, b=b: eng.dma_start(
                        out=qbb2[b][:], in_=q_s[(c - NCH // 2) * TW:(c - NCH // 2 + 1) * TW, :]),
                        reads=q_res, writes=(f"r_q{b}",), dma=1)
                sc.step("vector", lambda eng, c=c, b=b: eng.tensor_tensor(
                    ktl[:].rearrange("p (a b) -> p a b", b=HD),
                    kbb[b][:].rearrange("p (a b) -> p a b", b=HD),
                    sb_cof[:, c * H:(c + 1) * H].unsqueeze(2).broadcast_to((TW, H, HD)),
                    ALU.mult),
                    reads=(f"r_k{b}", "pg_cof"), writes=("r_kt",))
                for qtr in range(4):
                    def trk(eng, qtr=qtr):
                        rr = None
                        for j in range(4):
                            dc = qtr * 4 + j
                            rr = eng.matmul(pT[:, j * TW:(j + 1) * TW],
                                            ktl[:, dc * TW:(dc + 1) * TW], sb_identb[:],
                                            start=True, stop=True)
                        return rr
                    sc.step("tensor", trk, reads=("r_kt", "c_idb"), writes=("r_pT",))
                    sc.step("vector", lambda eng, qtr=qtr: eng.tensor_copy(
                        kTb[:, qtr * 4 * TW:(qtr + 1) * 4 * TW], pT[:]),
                        reads=("r_pT",), writes=(f"r_kT{qtr % 2}",))
                if own:
                    for qtr in range(4):
                        def trq2(eng, qtr=qtr, b=b):
                            rr = None
                            for j in range(4):
                                dc = qtr * 4 + j
                                rr = eng.matmul(pT[:, j * TW:(j + 1) * TW],
                                                qbb2[b][:, dc * TW:(dc + 1) * TW], sb_identb[:],
                                                start=True, stop=True)
                            return rr
                        sc.step("tensor", trq2, reads=(f"r_q{b}", "c_idb"), writes=("r_pT",))
                        sc.step("vector", lambda eng, qtr=qtr: eng.tensor_copy(
                            qTb[:, qtr * 4 * TW:(qtr + 1) * 4 * TW], pT[:]),
                            reads=("r_pT",), writes=(f"r_qT{qtr % 2}",))

                    for w in range(8):
                        par = w % 2
                        pg4 = w // 2

                        def msc(eng, par=par, pg4=pg4):
                            rr = None
                            for j in range(4):
                                hh = 8 * pg4 + 2 * j + par
                                dc = hh // 2
                                lo = (hh % 2) * 64
                                rr = eng.matmul(
                                    pS[:, j * TW:(j + 1) * TW],
                                    kTb[lo:lo + 64, dc * TW:(dc + 1) * TW],
                                    qTb[lo:lo + 64, dc * TW:(dc + 1) * TW],
                                    start=True, stop=True)
                            return rr
                        sc.step("tensor", msc, reads=("r_kT0", "r_kT1", "r_qT0", "r_qT1"),
                                writes=("r_pS",))

                        def evsc(eng, par=par, pg4=pg4):
                            dst = ATb[:].rearrange("p (x two w) -> p x two w", two=2, w=TW)[:, 4 * pg4:4 * pg4 + 4, par:par + 1, :]
                            mE = sb_maskb[:].unsqueeze(1).unsqueeze(2).broadcast_to((TW, 4, 1, TW))
                            src = pS[:].rearrange("p (x w) -> p x w", w=TW).unsqueeze(2)
                            return eng.tensor_tensor(dst, src, mE, ALU.mult)
                        sc.step("vector", evsc, reads=("r_pS", "c_maskb"), writes=(f"r_AT{w}",))
                    AT_res = tuple(f"r_AT{w}" for w in range(8))

                    for pg4 in range(4):
                        def mo(eng, pg4=pg4, b=b):
                            rr = None
                            for j in range(4):
                                p = 4 * pg4 + j
                                eng.matmul(pOb[pg4][:, j * TW:(j + 1) * TW],
                                           qTb[:, p * TW:(p + 1) * TW],
                                           Sb[:, p * TW:(p + 1) * TW],
                                           start=(j == 0), stop=False)
                                eng.matmul(pOb[pg4][:, j * TW:j * TW + HD],
                                           ATb[:, (2 * p) * TW:(2 * p + 1) * TW],
                                           vbb[b][:, (2 * p) * HD:(2 * p + 1) * HD],
                                           start=False, stop=False)
                                rr = eng.matmul(pOb[pg4][:, j * TW + HD:(j + 1) * TW],
                                                ATb[:, (2 * p + 1) * TW:(2 * p + 2) * TW],
                                                vbb[b][:, (2 * p + 1) * HD:(2 * p + 2) * HD],
                                                start=False, stop=(j == 3))
                            return rr
                        sc.step("tensor", mo,
                                reads=AT_res + ("r_qT0", "r_qT1", "r_Sb", f"r_v{b}"),
                                writes=(f"r_pO{pg4}",))

                for P in range(2):
                    for bank in range(2):
                        def mds(eng, P=P, bank=bank, b=b):
                            rr = None
                            for idx in range(4):
                                p = P * 8 + bank * 4 + idx
                                rr = eng.matmul(pDb[bank][:, idx * TW:(idx + 1) * TW],
                                                ktl[:, (2 * p) * HD:(2 * p + 2) * HD],
                                                vbb[b][:, (2 * p) * HD:(2 * p + 2) * HD],
                                                start=(idx == 0), stop=(idx == 3))
                            return rr
                        sc.step("tensor", mds, reads=("r_kt", f"r_v{b}"),
                                writes=(f"r_pD{bank}",))

                    def supd_pass(eng, P=P):
                        rr = None
                        for bank in range(2):
                            for half in range(2):
                                base = half * 64
                                sv = Sp[base:base + 64, :] \
                                    .rearrange("p (a w) -> p a w", w=TW)[
                                        :, P * 8 + bank * 4:P * 8 + bank * 4 + 4,
                                        base:base + 64]
                                dv = pDb[bank][base:base + 64, :] \
                                    .rearrange("p (a w) -> p a w", w=TW)[:, :, base:base + 64]
                                rr = eng.tensor_add(sv, sv, dv)
                        return rr
                    sc.step("vector", supd_pass,
                            reads=("r_pD0", "r_pD1", "r_Sp"), writes=("r_Sp",))

                if own:
                    for pg4 in range(4):
                        def evo(eng, c=c, pg4=pg4, b=b):
                            rofE = sb_rof[:, c * H + 8 * pg4:c * H + 8 * pg4 + 8] \
                                .unsqueeze(2).broadcast_to((TW, 8, HD))
                            return eng.tensor_tensor(
                                o1bb[b][:, pg4 * 512:(pg4 + 1) * 512].rearrange("p (a b) -> p a b", b=HD),
                                pOb[pg4][:].rearrange("p (a b) -> p a b", b=HD), rofE, ALU.mult)
                        sc.step("vector", evo, reads=(f"r_pO{pg4}",) + fac_res,
                                writes=(f"r_o{b}_{pg4}",))
                    sc.step("gpsimd", lambda eng, c=c, b=b: eng.dma_start(
                        out=o1_s[(c - NCH // 2) * TW:(c - NCH // 2 + 1) * TW, :], in_=o1bb[b][:]),
                        reads=tuple(f"r_o{b}_{g}" for g in range(4)), writes=("r_o1w",), dma=1, dstore=True)

                def supd_fin(eng, c=c):
                    q = lambda i: sc.seq("vector", eng, i)
                    for half in range(2):
                        base = half * 64
                        sv = Sp[base:base + 64, :].rearrange("p (a w) -> p a w", w=TW)[:, :, base:base + 64]
                        lamE = sb_lam[base:base + 64, c * H:(c + 1) * H] \
                            .rearrange("p (a two) -> p a two", two=2)[:, :, half:half + 1] \
                            .broadcast_to((64, NPAIR, HD))
                        q(eng.tensor_tensor(sv, sv, lamE, ALU.mult))
                    return eng.tensor_scalar(Sb[:], Sp[:], 0.125, None, ALU.mult)
                sc.step("vector", supd_fin,
                        reads=("pg_lam0", "pg_lam1", "r_Sp"),
                        writes=("r_Sp", "r_Sb"))
            sc.barrier()

        if PHASE_LIMIT <= 9:
            return nc
        with ExitStack() as wstk:
            we_ = wstk.enter_context
            wo_o0 = we_(nc.sbuf_tensor("w_o0", [TW, D], F32))
            wo_o1 = we_(nc.sbuf_tensor("w_o1", [TW, D], F32))
            wsg0 = we_(nc.sbuf_tensor("w_sg0", [TW, D], F32))
            wsg1 = we_(nc.sbuf_tensor("w_sg1", [TW, D], F32))
            wsq = we_(nc.sbuf_tensor("w_sq", [TW, D], F32))
            wvar = we_(nc.sbuf_tensor("w_var", [TW, 4 * H], F32))
            wob = we_(nc.sbuf_tensor("w_ob", [TW, D], BF16))
            oT = we_(nc.sbuf_tensor("w_oT", [TW, NT_OWN * D], BF16))
            wpan = we_(nc.sbuf_tensor("w_pan", [TW, NDC * 4 * TW], BF16))
            wwn0 = we_(nc.sbuf_tensor("w_wn0", [TW, D], BF16))
            wwn1 = we_(nc.sbuf_tensor("w_wn1", [TW, D], BF16))
            wxc0 = we_(nc.sbuf_tensor("w_xc0", [TW, 512], F32))
            wxc1 = we_(nc.sbuf_tensor("w_xc1", [TW, 512], F32))
            wx20 = we_(nc.sbuf_tensor("w_x20", [TW, 512], F32))
            wx21 = we_(nc.sbuf_tensor("w_x21", [TW, 512], F32))
            wpp0 = we_(nc.psum_tensor("w_pp0", [TW, 512], F32))
            wpp1 = we_(nc.psum_tensor("w_pp1", [TW, 512], F32))
            wpT = we_(nc.psum_tensor("w_pT", [TW, 512], F32))
            wtp0 = we_(nc.psum_tensor("w_tp0", [TW, 512], F32))
            wtp1 = we_(nc.psum_tensor("w_tp1", [TW, 512], F32))
            wobb, wsgb = [wo_o0, wo_o1], [wsg0, wsg1]
            wxcb, wx2b, wppb, wtpb = [wxc0, wxc1], [wx20, wx21], [wpp0, wpp1], [wtp0, wtp1]
            wwnb = [wwn0, wwn1]
            for t in range(NT_OWN):
                b = t % 2
                sc.step("sync", lambda eng, t=t, b=b: [
                    eng.dma_start(out=wobb[b][:], in_=o1_s[t * TW:(t + 1) * TW, :]),
                    eng.dma_start(out=wsgb[b][:], in_=sg_s[t * TW:(t + 1) * TW, :])],
                    reads=("r_o1w",) + sg_res, writes=(f"w_o{b}", f"w_sg{b}"), dma=2)

                def rms(eng, t=t, b=b):
                    q = lambda i: sc.seq("vector", eng, i)
                    q(eng.tensor_mul(wsq[:], wobb[b][:], wobb[b][:]))
                    q(eng.reduce_sum(wvar[:, 0:H], wsq[:].rearrange("p (a b) -> p a b", b=HD), axis=AXX))
                    q(eng.tensor_scalar(wvar[:, 0:H], wvar[:, 0:H], 1.0 / HD, EPS_RMS, ALU.mult, ALU.add))
                    return eng.reciprocal(wvar[:, H:2 * H], wvar[:, 0:H])
                sc.step("vector", rms, reads=(f"w_o{b}",), writes=("w_v1",))
                sc.step("scalar", lambda eng: eng.activation(
                    wvar[:, 2 * H:3 * H], wvar[:, H:2 * H], AF.Sqrt),
                    reads=("w_v1",), writes=("w_v2",))

                def gate(eng, b=b):
                    q = lambda i: sc.seq("vector", eng, i)
                    q(eng.tensor_mul(wvar[:, 3 * H:4 * H], wvar[:, 2 * H:3 * H], wvar[:, 2 * H:3 * H]))
                    q(eng.tensor_mul(wvar[:, 3 * H:4 * H], wvar[:, 3 * H:4 * H], wvar[:, 0:H]))
                    q(eng.tensor_scalar(wvar[:, 3 * H:4 * H], wvar[:, 3 * H:4 * H], -0.5, 1.5, ALU.mult, ALU.add))
                    q(eng.tensor_mul(wvar[:, 2 * H:3 * H], wvar[:, 2 * H:3 * H], wvar[:, 3 * H:4 * H]))
                    q(eng.tensor_mul(wsq[:], wobb[b][:], wsgb[b][:]))
                    rsE = wvar[:, 2 * H:3 * H].unsqueeze(2).broadcast_to((TW, H, HD))
                    return eng.tensor_tensor(wob[:].rearrange("p (a b) -> p a b", b=HD),
                                             wsq[:].rearrange("p (a b) -> p a b", b=HD),
                                             rsE, ALU.mult)
                sc.step("vector", gate, reads=("w_v2", f"w_o{b}", f"w_sg{b}"), writes=("w_ob",))
                for qtr in range(4):
                    def tro(eng, qtr=qtr):
                        rr = None
                        for j in range(4):
                            dc = qtr * 4 + j
                            rr = eng.matmul(wpT[:, j * TW:(j + 1) * TW],
                                            wob[:, dc * TW:(dc + 1) * TW], sb_identb[:],
                                            start=True, stop=True)
                        return rr
                    sc.step("tensor", tro, reads=("w_ob", "c_idb"), writes=("w_pT",))
                    sc.step("vector", lambda eng, t=t, qtr=qtr: eng.tensor_copy(
                        oT[:, (t * NDC + qtr * 4) * TW:(t * NDC + (qtr + 1) * 4) * TW], wpT[:]),
                        reads=("w_pT",), writes=(f"w_oT{t}h{qtr % 2}",))
            oT_res = tuple(f"w_oT{t}h{h}" for t in range(NT_OWN) for h in range(2))
            sc.flush()

            for ob in range(4):
                for ot in range(4):
                    rt = ob * 4 + ot
                    b = rt % 2
                    sc.step("gpsimd", lambda eng, rt=rt, b=b: eng.dma_start(
                        out=wwnb[b][:], in_=Wo[rt * TW:(rt + 1) * TW, :]),
                        writes=(f"w_wn{b}",), dma=1)
                    for g in range(4):
                        def trp(eng, b=b, g=g):
                            rr = None
                            for j in range(4):
                                dc = g * 4 + j
                                rr = eng.matmul(wtpb[g % 2][:, j * TW:(j + 1) * TW],
                                                wwnb[b][:, dc * TW:(dc + 1) * TW], sb_identb[:],
                                                start=True, stop=True)
                            return rr
                        sc.step("tensor", trp, reads=(f"w_wn{b}", "c_idb"), writes=(f"w_tp{g % 2}",))

                        def evp(eng, ot=ot, g=g):
                            rr = None
                            for j in range(4):
                                dc = g * 4 + j
                                rr = eng.tensor_copy(wpan[:, (dc * 4 + ot) * TW:(dc * 4 + ot + 1) * TW],
                                                     wtpb[g % 2][:, j * TW:(j + 1) * TW])
                            return rr
                        sc.step("vector", evp, reads=(f"w_tp{g % 2}",), writes=(f"w_pan{ot}g{g}",))
                panr = tuple(f"w_pan{o}g{g}" for o in range(4) for g in range(4))
                for t in range(NT_OWN):
                    b = t % 2
                    sc.step("sync", lambda eng, t=t, b=b, ob=ob: eng.dma_start(
                        out=wxcb[b][:], in_=x1v[OWN + t * TW:OWN + (t + 1) * TW,
                                                ob * 512:(ob + 1) * 512]),
                        reads=x1b_res, writes=(f"w_xc{b}",), dma=1)

                    def mmw(eng, t=t, b=b):
                        rr = None
                        for dc in range(NDC):
                            rr = eng.matmul(wppb[b][:],
                                            oT[:, (t * NDC + dc) * TW:(t * NDC + dc + 1) * TW],
                                            wpan[:, dc * 4 * TW:(dc + 1) * 4 * TW],
                                            start=(dc == 0), stop=(dc == NDC - 1))
                        return rr
                    sc.step("tensor", mmw, reads=oT_res + panr, writes=(f"w_pp{b}",))
                    sc.step("vector", lambda eng, b=b: eng.tensor_add(
                        wx2b[b][:], wppb[b][:], wxcb[b][:]),
                        reads=(f"w_pp{b}", f"w_xc{b}"), writes=(f"w_x2{b}",))
                    sc.step("gpsimd", lambda eng, t=t, ob=ob, b=b: eng.dma_start(
                        out=x2s[t * TW:(t + 1) * TW, ob * 512:(ob + 1) * 512], in_=wx2b[b][:]),
                        reads=(f"w_x2{b}",), writes=("x2w",), dma=1, dstore=True)
            sc.barrier()

        if PHASE_LIMIT <= 10:
            return nc
        out_res = ffn(x2s, out, "fC", "outw", in_res=("x2w",))
        sc.final_wait("sync", out_res)
        for E in ENGINES:
            assert not sc.pending[E]

    return nc


def _host_consts():
    ib = np.eye(TW, dtype=ml_dtypes.bfloat16)
    idf = np.eye(TW, dtype=np.float32)
    ones = np.ones((TW, TW), dtype=np.float32)
    jj, ii = np.meshgrid(np.arange(TW), np.arange(TW), indexing="ij")
    mb = ((jj <= ii) * 0.125).astype(ml_dtypes.bfloat16)
    mf = (jj <= ii).astype(np.float32)
    inv_freq = 1.0 / (10000.0 ** (np.arange(0, HD, 2, dtype=np.float32) / HD))
    return ib, idf, ones, mb, mf, inv_freq


def kernel(**inputs):
    x = np.asarray(inputs["L0_attn_hidden_states"], dtype=np.float32)
    B, T, _ = x.shape
    assert (B, T) == (4, 4096)
    names = dict(
        Wq="L0_attn_Wq", Wk="L0_attn_Wk", Wv="L0_attn_Wv", Wg="L0_attn_Wg",
        Wgk="L0_attn_Wgk", Wo="L0_attn_Wo", Wgate="L0_ffn_w_gate",
        Wup="L0_ffn_w_up", Wdown="L0_ffn_w_down")
    ws = {k: np.ascontiguousarray(np.asarray(inputs[v], dtype=np.float32))
          for k, v in names.items()}
    ib, idf, ones, mb, mf, inv_freq = _host_consts()

    if "nc" not in _CACHE:
        _CACHE["nc"] = build_program()
    nc = _CACHE["nc"]

    zeros_pre = np.zeros((OWN, D), dtype=np.float32)
    in_maps = []
    for c in range(8):
        bi, half = c // 2, c % 2
        own = np.ascontiguousarray(x[bi, half * OWN:(half + 1) * OWN])
        pre = zeros_pre if half == 0 else np.ascontiguousarray(x[bi, 0:OWN])
        pos = np.concatenate([np.arange(OWN), np.arange(half * OWN, (half + 1) * OWN)]) \
            .astype(np.float32)
        fr = pos[:, None] * inv_freq[None, :]
        im = dict(xpre=pre, xown=own, identb=ib, identf=idf, onesf=ones,
                  maskTb=mb, maskTf=mf,
                  cosc=np.cos(fr).astype(np.float32), sinc=np.sin(fr).astype(np.float32),
                  **ws)
        in_maps.append(im)

    res = run_bass_kernel_spmd(nc, in_maps, core_ids=list(range(8)))
    global _LAST_RESULTS
    _LAST_RESULTS = res.results
    outp = np.empty((B, T, D), dtype=np.float32)
    for c in range(8):
        bi, half = c // 2, c % 2
        outp[bi, half * OWN:(half + 1) * OWN] = res.results[c]["out"]
    return outp
